# version 01
# baseline (speedup 1.0000x reference)
import contextlib

import numpy as np
import ml_dtypes
import concourse.bass as bass
import concourse.mybir as mybir
from concourse.bass_utils import run_bass_kernel_spmd

F32 = mybir.dt.float32
BF16 = mybir.dt.bfloat16
I32 = mybir.dt.int32
AF = mybir.ActivationFunctionType
ALU = mybir.AluOpType
AX = mybir.AxisListType.X

B = 4
S = 4096
D = 1024
S_CORE = 2048
NT = S_CORE // 128
NDC = D // 128
MAGIC = float(np.float32(1.5 * 2 ** 23))
LN_EPS = 1e-5



_DRAIN_OPS = frozenset("""
tensor_scalar tensor_single_scalar tensor_scalar_add tensor_scalar_sub
tensor_scalar_mul tensor_scalar_max tensor_scalar_min tensor_relu
tensor_tensor tensor_add tensor_sub tensor_mul tensor_max tensor_copy
tensor_reduce reduce_max reduce_sum scalar_tensor_tensor tensor_tensor_scan
memset memzero activation sign square sqrt mul copy add reciprocal select
bn_stats bn_aggr transpose iota
""".split())


class DrainingEngine:

    def __init__(self, eng):
        self.raw = eng

    def __getattr__(self, name):
        attr = getattr(self.raw, name)
        if name in _DRAIN_OPS:
            def wrapped(*a, **k):
                self.raw.drain()
                return attr(*a, **k)
            return wrapped
        return attr


class DmaSlots:

    def __init__(self, nc, A, name, n):
        self.sems = [A(nc.semaphore(f"{name}{k}")) for k in range(n)]
        self.n = n

    def inc(self, inst, i):
        return inst.then_inc(self.sems[i % self.n], 16)

    def wait(self, eng, i):
        eng.wait_ge(self.sems[i % self.n], 16 * (i // self.n + 1))


def emit_weight_transpose(sync, dst, src, dsem, dcnt):
    ca = src.shape[1]
    cb = dst.shape[1]
    assert src.shape[2] == cb * 128 and dst.shape[2] == ca * 128
    for icb in range(cb):
        sync.dma_start(
            dst.ap()[:, icb, :].rearrange("p (ca q) -> p ca q", ca=ca),
            src.ap()[:, :, icb * 128:(icb + 1) * 128],
            transpose=True,
        ).then_inc(dsem, 16)
        dcnt += 16
    return dcnt


def emit_rstd(scalar, var_ap, rstd_ap):
    scalar.activation(rstd_ap, var_ap, AF.Ln)
    return scalar.activation(rstd_ap, rstd_ap, AF.Exp, scale=-0.5)


def emit_ln_stats_phase(blk, x_src_tiles, n_tiles, xt, sx_slots, ssq, ssum,
                        s_sumsq, s_sum, scr_sq, scr_cp, col0=0, cnt0=0):
    n_buf = xt.shape[1]

    @blk.gpsimd
    def _(g):
        for i in range(n_tiles):
            if i >= n_buf:
                g.wait_ge(ssq, cnt0 + i - n_buf + 1)
                g.wait_ge(ssum, cnt0 + i - n_buf + 1)
            sx_slots.inc(g.dma_start(xt[:, i % n_buf, :], x_src_tiles(i)), i)

    @blk.scalar
    def _(sc):
        sc = DrainingEngine(sc)
        for i in range(n_tiles):
            sx_slots.wait(sc, i)
            sc.activation(scr_sq[:, i % scr_sq.shape[1], :], xt[:, i % n_buf, :],
                          AF.Square,
                          accum_out=s_sumsq[:, col0 + i:col0 + i + 1]).then_inc(ssq, 1)

    @blk.vector
    def _(v):
        v = DrainingEngine(v)
        for i in range(n_tiles):
            sx_slots.wait(v, i)
            v.tensor_scalar(scr_cp[:, i % scr_cp.shape[1], :], xt[:, i % n_buf, :],
                            1.0, None, ALU.mult, ALU.add,
                            accum_out=s_sum[:, col0 + i:col0 + i + 1]).then_inc(ssum, 1)


def emit_stat_math_phase(blk, s_sum, s_sumsq, s_negm, s_var, s_rstd, width, dim):
    @blk.vector
    def _(v):
        v = DrainingEngine(v)
        v.tensor_scalar_mul(s_negm[:, 0:width], s_sum[:, 0:width], -1.0 / dim)
        v.tensor_tensor(s_var[:, 0:width], s_negm[:, 0:width], s_negm[:, 0:width], ALU.mult)
        v.tensor_scalar(s_sumsq[:, 0:width], s_sumsq[:, 0:width], 1.0 / dim, LN_EPS,
                        ALU.mult, ALU.add)
        v.tensor_tensor(s_var[:, 0:width], s_sumsq[:, 0:width], s_var[:, 0:width],
                        ALU.subtract)

    @blk.scalar
    def _(sc):
        pass


def emit_rstd_phase(blk, s_var, s_rstd, width):
    @blk.scalar
    def _(sc):
        emit_rstd(DrainingEngine(sc), s_var[:, 0:width], s_rstd[:, 0:width])




def emit_residual_phase(nc, A4, x_src_tiles, y_src_tiles, out_tiles, n_tiles,
                        xt, p_negm, p_rstd, stat0=0):
    yt = A4(nc.sbuf_tensor("res_yt", [128, 2, D], BF16))
    t1 = A4(nc.sbuf_tensor("res_t1", [128, 2, D], F32))
    xnew = A4(nc.sbuf_tensor("res_xnew", [128, 2, D], F32))
    sx4 = DmaSlots(nc, A4, "res_sx", 2)
    sy4 = DmaSlots(nc, A4, "res_sy", 2)
    sres = A4(nc.semaphore("res_done"))
    sst = DmaSlots(nc, A4, "res_st", 2)
    blk = A4(nc.Block())

    @blk.gpsimd
    def _(g):
        for i in range(n_tiles):
            if i >= 2:
                g.wait_ge(sres, i - 1)
            sx4.inc(g.dma_start(xt[:, i % 2, :], x_src_tiles(i)), i)
            sy4.inc(g.dma_start(yt[:, i % 2, :], y_src_tiles(i)), i)

    @blk.vector
    def _(v):
        v = DrainingEngine(v)
        for i in range(n_tiles):
            sx4.wait(v, i)
            sy4.wait(v, i)
            if i >= 2:
                sst.wait(v, i - 2)
            k = stat0 + i
            v.tensor_scalar(t1[:, i % 2, :], yt[:, i % 2, :],
                            p_negm[:, k:k + 1], p_rstd[:, k:k + 1],
                            ALU.add, ALU.mult)
            v.tensor_tensor(xnew[:, i % 2, :], xt[:, i % 2, :],
                            t1[:, i % 2, :], ALU.add).then_inc(sres, 1)

    @blk.sync
    def _(sy_):
        for i in range(n_tiles):
            sy_.wait_ge(sres, i + 1)
            sst.inc(sy_.dma_start(out_tiles(i), xnew[:, i % 2, :]), i)
        for k in range(2):
            sy_.wait_ge(sst.sems[k], 16 * ((n_tiles + 1 - k) // 2))




def build_l1_ffn():
    H = 3072
    NHC = H // 128
    NSL = H // 512
    c127 = float(np.float32(1.0 / 127.0))

    nc = bass.Bass("TRN2", target_bir_lowering=False)
    x_in = nc.dram_tensor("x_in", [S_CORE, D], F32, kind="ExternalInput")
    wg = nc.dram_tensor("wg", [H, D], I32, kind="ExternalInput")
    wu = nc.dram_tensor("wu", [H, D], I32, kind="ExternalInput")
    wd = nc.dram_tensor("wd", [D, H], I32, kind="ExternalInput")
    x_out = nc.dram_tensor("x_out", [S_CORE, D], F32, kind="ExternalOutput")
    y_scr = nc.dram_tensor("y_scr", [S_CORE, D], BF16)

    with contextlib.ExitStack() as es:
        A = es.enter_context
        wgT = A(nc.sbuf_tensor("wgT", [128, NHC, NDC, 128], BF16))
        wuT = A(nc.sbuf_tensor("wuT", [128, NHC, NDC, 128], BF16))
        wdT = A(nc.sbuf_tensor("wdT", [128, 2, NDC, NHC // 2, 128], BF16))
        xt = A(nc.sbuf_tensor("xt", [128, 2, D], F32))
        scr_sq = A(nc.sbuf_tensor("scr_sq", [128, 2, D], F32))
        scr_cp = A(nc.sbuf_tensor("scr_cp", [128, 2, D], BF16))
        s_sum = A(nc.sbuf_tensor("s_sum", [128, NT], F32))
        s_sumsq = A(nc.sbuf_tensor("s_sumsq", [128, NT], F32))
        s_negm = A(nc.sbuf_tensor("s_negm", [128, NT], F32))
        s_var = A(nc.sbuf_tensor("s_var", [128, NT], F32))
        s_rstd = A(nc.sbuf_tensor("s_rstd", [128, NT], F32))
        p_sum = A(nc.sbuf_tensor("p_sum", [128, NT], F32))
        p_sumsq = A(nc.sbuf_tensor("p_sumsq", [128, NT], F32))
        p_negm = A(nc.sbuf_tensor("p_negm", [128, NT], F32))
        p_var = A(nc.sbuf_tensor("p_var", [128, NT], F32))
        p_rstd = A(nc.sbuf_tensor("p_rstd", [128, NT], F32))
        dwg = A(nc.semaphore("dwg"))
        dws = A(nc.semaphore("dws"))
        sx = DmaSlots(nc, A, "sx", 2)
        ssq = A(nc.semaphore("ssq"))
        ssum = A(nc.semaphore("ssum"))
        gcnt = 0
        tcnt = 0
        with nc.sbuf_tensor("wraw", [128, NHC // 2, D], BF16) as wraw:
            for (w_dram, w_t) in ((wg, wgT), (wu, wuT)):
                for half in range(2):
                    hh = NHC // 2
                    with nc.Block() as blk:
                        @blk.gpsimd
                        def _(g, w_dram=w_dram, half=half, base=gcnt):
                            g.dma_start(
                                wraw[:],
                                w_dram.ap()[half * (H // 2):(half + 1) * (H // 2), :]
                                .rearrange("(c p) d -> p c d", p=128),
                            ).then_inc(dwg, 16)
                            g.wait_ge(dwg, base + 16)
                    gcnt += 16
                    with nc.Block() as blk:
                        @blk.sync
                        def _(sync, w_t=w_t, half=half, base=tcnt):
                            sync.dma_start(
                                w_t.ap()[:, half * hh:(half + 1) * hh, :, :],
                                wraw[:],
                                transpose=True,
                            ).then_inc(dws, 16)
                            sync.wait_ge(dws, base + 16)
                    tcnt += 16
        with nc.sbuf_tensor("wdraw", [128, NDC, H // 2], BF16) as wdraw:
            for half in range(2):
                with nc.Block() as blk:
                    @blk.gpsimd
                    def _(g, half=half, base=gcnt):
                        g.dma_start(
                            wdraw[:],
                            wd.ap()[:, half * (H // 2):(half + 1) * (H // 2)]
                            .rearrange("(c p) h -> p c h", p=128),
                        ).then_inc(dwg, 16)
                        g.wait_ge(dwg, base + 16)
                gcnt += 16
                with nc.Block() as blk:
                    @blk.sync
                    def _(sync, half=half, base=tcnt):
                        sync.dma_start(
                            wdT.ap()[:, half, :, :, :],
                            wdraw[:],
                            transpose=True,
                        ).then_inc(dws, 16)
                        sync.wait_ge(dws, base + 16)
                tcnt += 16

        with nc.Block() as blk:
            emit_ln_stats_phase(blk, lambda i: x_in.ap()[i * 128:(i + 1) * 128, :],
                                NT, xt, sx, ssq, ssum, s_sumsq, s_sum, scr_sq, scr_cp)
        with nc.Block() as blk:
            emit_stat_math_phase(blk, s_sum, s_sumsq, s_negm, s_var, s_rstd, NT, D)
        with nc.Block() as blk:
            emit_rstd_phase(blk, s_var, s_rstd, NT)

        with contextlib.ExitStack() as es3:
            A3 = es3.enter_context
            xln = A3(nc.sbuf_tensor("xln", [128, 2, D], BF16))
            xf = A3(nc.sbuf_tensor("xf", [128, 2, NDC, 128], BF16))
            gsil = A3(nc.sbuf_tensor("gsil", [128, 2, 512], BF16))
            usc = A3(nc.sbuf_tensor("usc", [128, 2, 512], BF16))
            h_tm = A3(nc.sbuf_tensor("h_tm", [128, 2, H], BF16))
            h_fm = A3(nc.sbuf_tensor("h_fm", [128, 2, NHC, 128], BF16))
            y_sb = A3(nc.sbuf_tensor("y_sb", [128, 2, D], BF16))
            sx3 = DmaSlots(nc, A3, "sx3", 2)
            sxln = A3(nc.semaphore("sxln"))
            sxf = DmaSlots(nc, A3, "sxf", 2)
            sgmm = A3(nc.semaphore("sgmm"))
            summ = A3(nc.semaphore("summ"))
            ssil = A3(nc.semaphore("ssil"))
            suev = A3(nc.semaphore("suev"))
            sh = A3(nc.semaphore("sh"))
            shf = DmaSlots(nc, A3, "shf", 2)
            sdmm = A3(nc.semaphore("sdmm"))
            spsq = A3(nc.semaphore("spsq"))
            sy2 = A3(nc.semaphore("sy2"))
            syst = DmaSlots(nc, A3, "syst", 2)
            gps = A3(nc.psum_tensor("gps", [128, 2, 512], F32))
            ups = A3(nc.psum_tensor("ups", [128, 2, 512], F32))
            dps = A3(nc.psum_tensor("dps", [128, 2, D], F32))
            blk = A3(nc.Block())
            @blk.gpsimd
            def _(g):
                for i in range(NT):
                    if i >= 2:
                        g.wait_ge(sxln, i - 1)
                    sx3.inc(g.dma_start(xt[:, i % 2, :],
                                        x_in.ap()[i * 128:(i + 1) * 128, :]), i)

            @blk.vector
            def _(v):
                v = DrainingEngine(v)
                for i in range(NT):
                    sx3.wait(v, i)
                    if i >= 2:
                        sxf.wait(v, i - 2)
                    v.tensor_scalar(xln[:, i % 2, :], xt[:, i % 2, :],
                                    s_negm[:, i:i + 1], s_rstd[:, i:i + 1],
                                    ALU.add, ALU.mult).then_inc(sxln, 1)
                    for j in range(NSL):
                        gidx = i * NSL + j
                        v.wait_ge(summ, gidx + 1)
                        v.tensor_scalar_mul(usc[:, j % 2, :], ups[:, j % 2, :],
                                            c127).then_inc(suev, 1)
                        v.wait_ge(ssil, gidx + 1)
                        if i >= 2 and j == 0:
                            shf.wait(v, i - 2)
                        v.tensor_tensor(h_tm[:, i % 2, j * 512:(j + 1) * 512],
                                        gsil[:, j % 2, :], usc[:, j % 2, :],
                                        ALU.mult).then_inc(sh, 1)
                    v.wait_ge(sdmm, i + 1)
                    if i >= 2:
                        syst.wait(v, i - 2)
                    v.tensor_scalar(y_sb[:, i % 2, :], dps[:, i % 2, :], c127, None,
                                    ALU.mult, ALU.add,
                                    accum_out=p_sum[:, i:i + 1]).then_inc(sy2, 1)

            @blk.scalar
            def _(sc):
                sc = DrainingEngine(sc)
                for i in range(NT):
                    for j in range(NSL):
                        gidx = i * NSL + j
                        sc.wait_ge(sgmm, gidx + 1)
                        if gidx >= 2:
                            sc.wait_ge(sh, gidx - 1)
                        sc.activation(gsil[:, j % 2, :], gps[:, j % 2, :], AF.Silu,
                                      scale=c127).then_inc(ssil, 1)
                    sc.wait_ge(sy2, i + 1)
                    sc.activation(scr_sq[:, i % 2, :], y_sb[:, i % 2, :], AF.Square,
                                  accum_out=p_sumsq[:, i:i + 1]).then_inc(spsq, 1)

            @blk.sync
            def _(sy_):
                for i in range(NT):
                    sy_.wait_ge(sxln, i + 1)
                    sxf.inc(sy_.dma_start(
                        xf.ap()[:, i % 2, :, :],
                        xln.ap()[:, i % 2, :].rearrange("p (c q) -> p c q", c=NDC),
                        transpose=True,
                    ), i)
                    sy_.wait_ge(sh, (i + 1) * NSL)
                    shf.inc(sy_.dma_start(
                        h_fm.ap()[:, i % 2, :, :],
                        h_tm.ap()[:, i % 2, :].rearrange("p (c q) -> p c q", c=NHC),
                        transpose=True,
                    ), i)
                    sy_.wait_ge(sy2, i + 1)
                    syst.inc(sy_.dma_start(y_scr.ap()[i * 128:(i + 1) * 128, :],
                                           y_sb[:, i % 2, :]), i)
                for k in range(2):
                    sy_.wait_ge(syst.sems[k], 16 * (NT // 2))

            @blk.tensor
            def _(t):
                for i in range(NT):
                    sxf.wait(t, i)
                    for j in range(NSL):
                        gidx = i * NSL + j
                        if gidx >= 2:
                            t.wait_ge(ssil, gidx - 1)
                        for c in range(NDC):
                            t.matmul(gps[:, j % 2, :],
                                     xf[:, i % 2, c, :],
                                     wgT.ap()[:, 4 * j:4 * j + 4, c, :],
                                     start=(c == 0), stop=(c == NDC - 1))
                        t.drain().then_inc(sgmm, 1)
                        if gidx >= 2:
                            t.wait_ge(suev, gidx - 1)
                        for c in range(NDC):
                            t.matmul(ups[:, j % 2, :],
                                     xf[:, i % 2, c, :],
                                     wuT.ap()[:, 4 * j:4 * j + 4, c, :],
                                     start=(c == 0), stop=(c == NDC - 1))
                        t.drain().then_inc(summ, 1)
                    shf.wait(t, i)
                    if i >= 2:
                        t.wait_ge(sy2, i - 1)
                        t.wait_ge(spsq, i - 1)
                    for n2 in range(2):
                        for c in range(NHC):
                            t.matmul(dps[:, i % 2, n2 * 512:(n2 + 1) * 512],
                                     h_fm[:, i % 2, c, :],
                                     wdT.ap()[:, c // (NHC // 2), 4 * n2:4 * n2 + 4,
                                              c % (NHC // 2), :],
                                     start=(c == 0), stop=(c == NHC - 1))
                    t.drain().then_inc(sdmm, 1)

        with nc.Block() as blk:
            emit_stat_math_phase(blk, p_sum, p_sumsq, p_negm, p_var, p_rstd, NT, D)
        with nc.Block() as blk:
            emit_rstd_phase(blk, p_var, p_rstd, NT)
        with contextlib.ExitStack() as es4:
            A4 = es4.enter_context
            yt = A4(nc.sbuf_tensor("yt", [128, 2, D], BF16))
            t1 = A4(nc.sbuf_tensor("t1", [128, 2, D], F32))
            xnew = A4(nc.sbuf_tensor("xnew", [128, 2, D], F32))
            sx4 = DmaSlots(nc, A4, "sx4", 2)
            sy4 = DmaSlots(nc, A4, "sy4", 2)
            sres = A4(nc.semaphore("sres"))
            sst = DmaSlots(nc, A4, "sst", 2)
            blk = A4(nc.Block())
            @blk.gpsimd
            def _(g):
                for i in range(NT):
                    if i >= 2:
                        g.wait_ge(sres, i - 1)
                    sx4.inc(g.dma_start(xt[:, i % 2, :],
                                        x_in.ap()[i * 128:(i + 1) * 128, :]), i)
                    sy4.inc(g.dma_start(yt[:, i % 2, :],
                                        y_scr.ap()[i * 128:(i + 1) * 128, :]), i)

            @blk.vector
            def _(v):
                v = DrainingEngine(v)
                for i in range(NT):
                    sx4.wait(v, i)
                    sy4.wait(v, i)
                    if i >= 2:
                        sst.wait(v, i - 2)
                    v.tensor_scalar(t1[:, i % 2, :], yt[:, i % 2, :],
                                    p_negm[:, i:i + 1], p_rstd[:, i:i + 1],
                                    ALU.add, ALU.mult)
                    v.tensor_tensor(xnew[:, i % 2, :], xt[:, i % 2, :],
                                    t1[:, i % 2, :], ALU.add).then_inc(sres, 1)

            @blk.sync
            def _(sy_):
                for i in range(NT):
                    sy_.wait_ge(sres, i + 1)
                    sst.inc(sy_.dma_start(x_out.ap()[i * 128:(i + 1) * 128, :],
                                          xnew[:, i % 2, :]), i)
                for k in range(2):
                    sy_.wait_ge(sst.sems[k], 16 * (NT // 2))

    return nc






def build_l0_ffn():
    H = 4096
    NHC = H // 128
    NSL = H // 512
    NQ = 4
    c127 = float(np.float32(1.0 / 127.0))

    nc = bass.Bass("TRN2", target_bir_lowering=False)
    x_in = nc.dram_tensor("x_in", [S_CORE, D], F32, kind="ExternalInput")
    wg = nc.dram_tensor("wg", [H, D], F32, kind="ExternalInput")
    wu = nc.dram_tensor("wu", [H, D], F32, kind="ExternalInput")
    wd = nc.dram_tensor("wd", [D, H], F32, kind="ExternalInput")
    x_out = nc.dram_tensor("x_out", [S_CORE, D], F32, kind="ExternalOutput")
    y_scr = nc.dram_tensor("y_scr", [S_CORE, D], BF16)
    inth_scr = nc.dram_tensor("inth_scr", [S_CORE, H], BF16)

    QF = H * D // NQ // 128

    with contextlib.ExitStack() as es:
        A = es.enter_context
        xt = A(nc.sbuf_tensor("xt", [128, 2, D], F32))
        scr_sq = A(nc.sbuf_tensor("scr_sq", [128, 2, D], F32))
        scr_cp = A(nc.sbuf_tensor("scr_cp", [128, 2, D], BF16))
        s_sum = A(nc.sbuf_tensor("s_sum", [128, NT], F32))
        s_sumsq = A(nc.sbuf_tensor("s_sumsq", [128, NT], F32))
        s_negm = A(nc.sbuf_tensor("s_negm", [128, NT], F32))
        s_var = A(nc.sbuf_tensor("s_var", [128, NT], F32))
        s_rstd = A(nc.sbuf_tensor("s_rstd", [128, NT], F32))
        p_sum = A(nc.sbuf_tensor("p_sum", [128, NT], F32))
        p_sumsq = A(nc.sbuf_tensor("p_sumsq", [128, NT], F32))
        p_negm = A(nc.sbuf_tensor("p_negm", [128, NT], F32))
        p_var = A(nc.sbuf_tensor("p_var", [128, NT], F32))
        p_rstd = A(nc.sbuf_tensor("p_rstd", [128, NT], F32))
        s_a = A(nc.sbuf_tensor("s_a", [128, NT], F32))
        s_cg = A(nc.sbuf_tensor("s_cg", [128, NT], F32))
        s_cu = A(nc.sbuf_tensor("s_cu", [128, NT], F32))
        s_a2 = A(nc.sbuf_tensor("s_a2", [128, NT], F32))
        s_c2 = A(nc.sbuf_tensor("s_c2", [128, NT], F32))
        wm_b = A(nc.sbuf_tensor("wm_b", [128, 6], F32))
        wsum = A(nc.sbuf_tensor("wsum", [128, 3 * NQ], F32))
        ones_c = A(nc.sbuf_tensor("ones_c", [128, 1], F32))
        ones_r = A(nc.sbuf_tensor("ones_r", [1, 128], F32))
        row6 = A(nc.sbuf_tensor("row6", [1, 6], F32))
        rowq = A(nc.sbuf_tensor("rowq", [1, 3 * NQ], F32))

        with contextlib.ExitStack() as es0:
            A0 = es0.enter_context
            wst = A0(nc.sbuf_tensor("wst", [128, 2, QF], F32))
            sld = DmaSlots(nc, A0, "b_ld", 2)
            sred = A0(nc.semaphore("b_red"))
            blk = A0(nc.Block())
            srcs = []
            for widx, w_dram in enumerate((wg, wu)):
                for q in range(NQ):
                    srcs.append(w_dram.ap()[q * (H // NQ):(q + 1) * (H // NQ), :]
                                .rearrange("(c p) d -> p c d", p=128))
            for q in range(NQ):
                srcs.append(wd.ap()[q * (D // NQ):(q + 1) * (D // NQ), :]
                            .rearrange("(c p) h -> p c h", p=128))

            @blk.gpsimd
            def _(g):
                for k in range(3 * NQ):
                    if k >= 2:
                        g.wait_ge(sred, k - 1)
                    sld.inc(g.dma_start(
                        wst.ap()[:, k % 2, :].rearrange("p (c d) -> p c d", d=D if k < 2 * NQ else H),
                        srcs[k]), k)

            @blk.vector
            def _(v):
                v = DrainingEngine(v)
                for k in range(3 * NQ):
                    sld.wait(v, k)
                    v.reduce_sum(wsum[:, k:k + 1], wst[:, k % 2, :], axis=AX,
                                 apply_absolute_value=True).then_inc(sred, 1)

        with contextlib.ExitStack() as es0:
            A0 = es0.enter_context
            pst = A0(nc.psum_tensor("b_pst", [128, 16], F32))
            with nc.Block() as blk:
                @blk.vector
                def _(v):
                    v.memset(ones_c[:], 1.0)
                    v.memset(ones_r[:], 1.0)
            with nc.Block() as blk:
                @blk.tensor
                def _(t):
                    t.matmul(pst[0:1, 0:3 * NQ], ones_c[:], wsum[:], start=True, stop=True)
                    t.drain()
            with nc.Block() as blk:
                @blk.vector
                def _(v):
                    v = DrainingEngine(v)
                    v.tensor_copy(rowq[:], pst[0:1, 0:3 * NQ])
                    for w in range(3):
                        v.reduce_sum(row6[:, w:w + 1], rowq[:, w * NQ:(w + 1) * NQ], axis=AX)
                    v.tensor_scalar(row6[:, 0:3], row6[:, 0:3], 1.0 / (H * D), 1e-5,
                                    ALU.mult, ALU.max)
                    v.reciprocal(row6[:, 3:6], row6[:, 0:3])
            with nc.Block() as blk:
                @blk.tensor
                def _(t):
                    t.matmul(pst[:, 0:6], ones_r[:], row6[:], start=True, stop=True)
                    t.drain()
            with nc.Block() as blk:
                @blk.vector
                def _(v):
                    v.tensor_copy(wm_b[:], pst[:, 0:6])

        def emit_quant_weights(A0, pairs, stp_name):
            wstq = A0(nc.sbuf_tensor(stp_name + "_f", [128, QF], F32))
            wqb = A0(nc.sbuf_tensor(stp_name + "_b", [128, QF], BF16))
            sld = A0(nc.semaphore(stp_name + "_ld"))
            sqn = A0(nc.semaphore(stp_name + "_qn"))
            stp = A0(nc.semaphore(stp_name + "_tp"))
            blk = A0(nc.Block())

            @blk.gpsimd
            def _(g):
                for k, (src, dst, widx) in enumerate(pairs):
                    if k >= 1:
                        g.wait_ge(sqn, k)
                    g.dma_start(wstq.ap().rearrange("p (c d) -> p c d", c=src.shape[1]),
                                src).then_inc(sld, 16)

            @blk.vector
            def _(v):
                v = DrainingEngine(v)
                for k, (src, dst, widx) in enumerate(pairs):
                    v.wait_ge(sld, 16 * (k + 1))
                    if k >= 1:
                        v.wait_ge(stp, 16 * k)
                    v.tensor_scalar(wstq[:], wstq[:], wm_b[:, 3 + widx:4 + widx],
                                    MAGIC, ALU.mult, ALU.add)
                    v.tensor_scalar(wstq[:], wstq[:], MAGIC, 1.0,
                                    ALU.subtract, ALU.min)
                    v.tensor_scalar(wqb[:], wstq[:], -1.0, None,
                                    ALU.max, ALU.bypass).then_inc(sqn, 1)

            @blk.sync
            def _(sy_):
                for k, (src, dst, widx) in enumerate(pairs):
                    sy_.wait_ge(sqn, k + 1)
                    sy_.dma_start(dst, wstq if False else wqb[:],
                                  transpose=True).then_inc(stp, 16)
                sy_.wait_ge(stp, 16 * len(pairs))

        sx = DmaSlots(nc, A, "sx", 2)
        ssq = A(nc.semaphore("ssq"))
        ssum = A(nc.semaphore("ssum"))
        with nc.Block() as blk:
            emit_ln_stats_phase(blk, lambda i: x_in.ap()[i * 128:(i + 1) * 128, :],
                                NT, xt, sx, ssq, ssum, s_sumsq, s_sum, scr_sq, scr_cp)
        with nc.Block() as blk:
            emit_stat_math_phase(blk, s_sum, s_sumsq, s_negm, s_var, s_rstd, NT, D)
        with nc.Block() as blk:
            emit_rstd_phase(blk, s_var, s_rstd, NT)

        with contextlib.ExitStack() as es3:
            A3 = es3.enter_context
            wgT = A3(nc.sbuf_tensor("wgT", [128, NHC, NDC, 128], BF16))
            wuT = A3(nc.sbuf_tensor("wuT", [128, NHC, NDC, 128], BF16))
            with contextlib.ExitStack() as esq:
                pairs = []
                for widx, (w_dram, w_t) in enumerate(((wg, wgT), (wu, wuT))):
                    for q in range(NQ):
                        src = (w_dram.ap()[q * (H // NQ):(q + 1) * (H // NQ), :]
                               .rearrange("(c p) d -> p c d", p=128))
                        dst = w_t.ap()[:, q * (NHC // NQ):(q + 1) * (NHC // NQ), :, :]
                        pairs.append((src, dst, widx))
                emit_quant_weights(esq.enter_context, pairs, "qgu")

            xln = A3(nc.sbuf_tensor("xln", [128, D], F32))
            scrq = scr_sq
            intx = A3(nc.sbuf_tensor("intx", [128, 2, D], BF16))
            ixf = A3(nc.sbuf_tensor("ixf", [128, 2, NDC, 128], BF16))
            gsil = A3(nc.sbuf_tensor("gsil", [128, 2, 512], BF16))
            usc = A3(nc.sbuf_tensor("usc", [128, 2, 512], BF16))
            h_tm = A3(nc.sbuf_tensor("h_tm", [128, H], BF16))
            inth = A3(nc.sbuf_tensor("inth", [128, 2, H], BF16))
            scrh = A3(nc.sbuf_tensor("scrh", [128, H // 2], F32))
            sx3 = DmaSlots(nc, A3, "sx3", 2)
            sxln = A3(nc.semaphore("sxln"))
            sixf = DmaSlots(nc, A3, "sixf", 2)
            sgmm = A3(nc.semaphore("sgmm"))
            summ = A3(nc.semaphore("summ"))
            ssil = A3(nc.semaphore("ssil"))
            suev = A3(nc.semaphore("suev"))
            sh = A3(nc.semaphore("sh"))
            sqh = A3(nc.semaphore("sqh"))
            shst = DmaSlots(nc, A3, "shst", 2)
            gps = A3(nc.psum_tensor("gps", [128, 2, 512], F32))
            ups = A3(nc.psum_tensor("ups", [128, 2, 512], F32))
            blk = A3(nc.Block())

            @blk.gpsimd
            def _(g):
                for i in range(NT):
                    if i >= 2:
                        g.wait_ge(sxln, 2 * (i - 1))
                    sx3.inc(g.dma_start(xt[:, i % 2, :],
                                        x_in.ap()[i * 128:(i + 1) * 128, :]), i)

            @blk.vector
            def _(v):
                v = DrainingEngine(v)
                for i in range(NT):
                    sx3.wait(v, i)
                    if i >= 2:
                        sixf.wait(v, i - 2)
                    v.tensor_scalar(xln[:], xt[:, i % 2, :],
                                    s_negm[:, i:i + 1], s_rstd[:, i:i + 1],
                                    ALU.add, ALU.mult).then_inc(sxln, 1)
                    v.reduce_max(s_a[:, i:i + 1], xln[:], axis=AX,
                                 apply_absolute_value=True)
                    v.tensor_scalar_max(s_a[:, i:i + 1], s_a[:, i:i + 1], 1e-5)
                    v.tensor_scalar(s_cg[:, i:i + 1], s_a[:, i:i + 1], c127, None,
                                    ALU.mult, ALU.bypass)
                    v.tensor_tensor(s_cu[:, i:i + 1], s_cg[:, i:i + 1],
                                    wm_b[:, 1:2], ALU.mult)
                    v.tensor_tensor(s_cg[:, i:i + 1], s_cg[:, i:i + 1],
                                    wm_b[:, 0:1], ALU.mult)
                    v.reciprocal(s_a[:, i:i + 1], s_a[:, i:i + 1])
                    v.tensor_scalar_mul(s_a[:, i:i + 1], s_a[:, i:i + 1], 127.0)
                    v.tensor_scalar(scrq[:, i % 2, :], xln[:],
                                    s_a[:, i:i + 1], MAGIC, ALU.mult, ALU.add)
                    v.tensor_scalar(scrq[:, i % 2, :], scrq[:, i % 2, :],
                                    MAGIC, 127.0, ALU.subtract, ALU.min)
                    v.tensor_scalar(intx[:, i % 2, :], scrq[:, i % 2, :],
                                    -128.0, None, ALU.max,
                                    ALU.bypass).then_inc(sxln, 1)
                    for j in range(NSL):
                        gidx = i * NSL + j
                        v.wait_ge(summ, gidx + 1)
                        v.tensor_scalar_mul(usc[:, j % 2, :], ups[:, j % 2, :],
                                            s_cu[:, i:i + 1]).then_inc(suev, 1)
                        v.wait_ge(ssil, gidx + 1)
                        if i >= 2 and j == 0:
                            v.wait_ge(sqh, i - 1)
                        v.tensor_tensor(h_tm[:, j * 512:(j + 1) * 512],
                                        gsil[:, j % 2, :], usc[:, j % 2, :],
                                        ALU.mult).then_inc(sh, 1)
                    v.reduce_max(s_a2[:, i:i + 1], h_tm[:], axis=AX,
                                 apply_absolute_value=True)
                    v.tensor_scalar_max(s_a2[:, i:i + 1], s_a2[:, i:i + 1], 1e-5)
                    v.tensor_scalar(s_c2[:, i:i + 1], s_a2[:, i:i + 1], c127, None,
                                    ALU.mult, ALU.bypass)
                    v.tensor_tensor(s_c2[:, i:i + 1], s_c2[:, i:i + 1],
                                    wm_b[:, 2:3], ALU.mult)
                    v.reciprocal(s_a2[:, i:i + 1], s_a2[:, i:i + 1])
                    v.tensor_scalar_mul(s_a2[:, i:i + 1], s_a2[:, i:i + 1], 127.0)
                    if i >= 2:
                        shst.wait(v, i - 2)
                    for hv in range(2):
                        hs = slice(hv * (H // 2), (hv + 1) * (H // 2))
                        v.tensor_scalar(scrh[:], h_tm[:, hs],
                                        s_a2[:, i:i + 1], MAGIC, ALU.mult, ALU.add)
                        v.tensor_scalar(scrh[:], scrh[:], MAGIC, 127.0,
                                        ALU.subtract, ALU.min)
                        v.tensor_scalar(inth[:, i % 2, hs], scrh[:], -128.0, None,
                                        ALU.max, ALU.bypass)
                    v.raw.drain().then_inc(sqh, 1)

            @blk.scalar
            def _(sc):
                sc = DrainingEngine(sc)
                for i in range(NT):
                    for j in range(NSL):
                        gidx = i * NSL + j
                        sc.wait_ge(sgmm, gidx + 1)
                        if gidx >= 2:
                            sc.wait_ge(sh, gidx - 1)
                        sc.activation(gsil[:, j % 2, :], gps[:, j % 2, :], AF.Silu,
                                      scale=s_cg[:, i:i + 1]).then_inc(ssil, 1)

            @blk.sync
            def _(sy_):
                for i in range(NT):
                    sy_.wait_ge(sxln, 2 * (i + 1))
                    sixf.inc(sy_.dma_start(
                        ixf.ap()[:, i % 2, :, :],
                        intx.ap()[:, i % 2, :].rearrange("p (c q) -> p c q", c=NDC),
                        transpose=True,
                    ), i)
                    sy_.wait_ge(sqh, i + 1)
                    shst.inc(sy_.dma_start(inth_scr.ap()[i * 128:(i + 1) * 128, :],
                                           inth[:, i % 2, :]), i)
                for k in range(2):
                    sy_.wait_ge(shst.sems[k], 16 * ((NT + 1 - k) // 2))

            @blk.tensor
            def _(t):
                for i in range(NT):
                    sixf.wait(t, i)
                    for j in range(NSL):
                        gidx = i * NSL + j
                        if gidx >= 2:
                            t.wait_ge(ssil, gidx - 1)
                        for c in range(NDC):
                            t.matmul(gps[:, j % 2, :],
                                     ixf[:, i % 2, c, :],
                                     wgT.ap()[:, 4 * j:4 * j + 4, c, :],
                                     start=(c == 0), stop=(c == NDC - 1))
                        t.drain().then_inc(sgmm, 1)
                        if gidx >= 2:
                            t.wait_ge(suev, gidx - 1)
                        for c in range(NDC):
                            t.matmul(ups[:, j % 2, :],
                                     ixf[:, i % 2, c, :],
                                     wuT.ap()[:, 4 * j:4 * j + 4, c, :],
                                     start=(c == 0), stop=(c == NDC - 1))
                        t.drain().then_inc(summ, 1)

        with contextlib.ExitStack() as es3:
            A3 = es3.enter_context
            wdT = A3(nc.sbuf_tensor("wdT", [128, NQ, D // NQ // 128, NHC, 128], BF16))
            with contextlib.ExitStack() as esq:
                pairs = []
                for q in range(NQ):
                    src = (wd.ap()[q * (D // NQ):(q + 1) * (D // NQ), :]
                           .rearrange("(c p) h -> p c h", p=128))
                    dst = wdT.ap()[:, q, :, :, :]
                    pairs.append((src, dst, 2))
                emit_quant_weights(esq.enter_context, pairs, "qd")

            ihf = A3(nc.sbuf_tensor("ihf", [128, 2, NHC, 128], BF16))
            y_sb = A3(nc.sbuf_tensor("y_sb", [128, 2, D], BF16))
            sihf = DmaSlots(nc, A3, "sihf", 2)
            sdmm = A3(nc.semaphore("sdmm"))
            sy2 = A3(nc.semaphore("sy2"))
            spsq = A3(nc.semaphore("spsq"))
            syst = DmaSlots(nc, A3, "syst", 2)
            dps = A3(nc.psum_tensor("dps", [128, 2, D], F32))
            blk = A3(nc.Block())

            @blk.sync
            def _(sy_):
                for i in range(NT):
                    if i >= 2:
                        sy_.wait_ge(sdmm, i - 1)
                    sihf.inc(sy_.dma_start(
                        ihf.ap()[:, i % 2, :, :],
                        inth_scr.ap()[i * 128:(i + 1) * 128, :]
                        .rearrange("p (c q) -> p c q", c=NHC),
                        transpose=True,
                    ), i)
                    sy_.wait_ge(sy2, i + 1)
                    syst.inc(sy_.dma_start(y_scr.ap()[i * 128:(i + 1) * 128, :],
                                           y_sb[:, i % 2, :]), i)
                for k in range(2):
                    sy_.wait_ge(syst.sems[k], 16 * ((NT + 1 - k) // 2))

            @blk.tensor
            def _(t):
                for i in range(NT):
                    sihf.wait(t, i)
                    if i >= 2:
                        t.wait_ge(sy2, i - 1)
                    for n2 in range(2):
                        for c in range(NHC):
                            t.matmul(dps[:, i % 2, n2 * 512:(n2 + 1) * 512],
                                     ihf[:, i % 2, c, :],
                                     wdT.ap()[:, 2 * n2:2 * n2 + 2, :, c, :]
                                     .rearrange("p a b q -> p (a b) q"),
                                     start=(c == 0), stop=(c == NHC - 1))
                    t.drain().then_inc(sdmm, 1)

            @blk.vector
            def _(v):
                v = DrainingEngine(v)
                for i in range(NT):
                    v.wait_ge(sdmm, i + 1)
                    if i >= 2:
                        syst.wait(v, i - 2)
                        v.wait_ge(spsq, i - 1)
                    v.tensor_scalar(y_sb[:, i % 2, :], dps[:, i % 2, :],
                                    s_c2[:, i:i + 1], None, ALU.mult, ALU.add,
                                    accum_out=p_sum[:, i:i + 1]).then_inc(sy2, 1)

            @blk.scalar
            def _(sc):
                sc = DrainingEngine(sc)
                for i in range(NT):
                    sc.wait_ge(sy2, i + 1)
                    sc.activation(scr_sq[:, i % 2, :], y_sb[:, i % 2, :], AF.Square,
                                  accum_out=p_sumsq[:, i:i + 1]).then_inc(spsq, 1)

        with nc.Block() as blk:
            emit_stat_math_phase(blk, p_sum, p_sumsq, p_negm, p_var, p_rstd, NT, D)
        with nc.Block() as blk:
            emit_rstd_phase(blk, p_var, p_rstd, NT)
        with contextlib.ExitStack() as es4:
            emit_residual_phase(nc, es4.enter_context,
                                lambda i: x_in.ap()[i * 128:(i + 1) * 128, :],
                                lambda i: y_scr.ap()[i * 128:(i + 1) * 128, :],
                                lambda i: x_out.ap()[i * 128:(i + 1) * 128, :],
                                NT, xt, p_negm, p_rstd)

    return nc




def build_l0_attn():
    NH = 16
    NP = 8
    NTF = 2 * NT
    EPS = 1e-6

    nc = bass.Bass("TRN2", target_bir_lowering=False)
    x_in = nc.dram_tensor("x_in", [S_CORE, D], F32, kind="ExternalInput")
    x_oth = nc.dram_tensor("x_oth", [S_CORE, D], F32, kind="ExternalInput")
    wqd = nc.dram_tensor("wq", [D, D], F32, kind="ExternalInput")
    wkd = nc.dram_tensor("wk", [D, D], F32, kind="ExternalInput")
    wvd = nc.dram_tensor("wv", [D, D], F32, kind="ExternalInput")
    wod = nc.dram_tensor("wo", [D, D], F32, kind="ExternalInput")
    bod = nc.dram_tensor("bo", [1, D], F32, kind="ExternalInput")
    x_out = nc.dram_tensor("x_out", [S_CORE, D], F32, kind="ExternalOutput")
    y_scr = nc.dram_tensor("y_scr", [S_CORE, D], BF16)

    def tile_src(t):
        if t < NT:
            return x_in.ap()[t * 128:(t + 1) * 128, :]
        return x_oth.ap()[(t - NT) * 128:(t - NT + 1) * 128, :]

    with contextlib.ExitStack() as es:
        A = es.enter_context
        wqT = A(nc.sbuf_tensor("wqT", [128, NDC, NDC, 128], BF16))
        wkT = A(nc.sbuf_tensor("wkT", [128, NDC, NDC, 128], BF16))
        wvT = A(nc.sbuf_tensor("wvT", [128, NDC, NDC, 128], BF16))
        woT = A(nc.sbuf_tensor("woT", [128, NDC, NDC, 128], BF16))
        bo_bf = A(nc.sbuf_tensor("bo_bf", [1, D], BF16))
        ones_r = A(nc.sbuf_tensor("ones_r", [1, 128], BF16))
        ones_c = A(nc.sbuf_tensor("ones_c", [128, 1], BF16))
        xt = A(nc.sbuf_tensor("xt", [128, 2, D], F32))
        scr_sq = A(nc.sbuf_tensor("scr_sq", [128, 2, D], F32))
        scr_cp = A(nc.sbuf_tensor("scr_cp", [128, 2, D], BF16))
        s_sum = A(nc.sbuf_tensor("s_sum", [128, NTF], F32))
        s_sumsq = A(nc.sbuf_tensor("s_sumsq", [128, NTF], F32))
        s_negm = A(nc.sbuf_tensor("s_negm", [128, NTF], F32))
        s_var = A(nc.sbuf_tensor("s_var", [128, NTF], F32))
        s_rstd = A(nc.sbuf_tensor("s_rstd", [128, NTF], F32))
        p_sum = A(nc.sbuf_tensor("p_sum", [128, NT], F32))
        p_sumsq = A(nc.sbuf_tensor("p_sumsq", [128, NT], F32))
        p_negm = A(nc.sbuf_tensor("p_negm", [128, NT], F32))
        p_var = A(nc.sbuf_tensor("p_var", [128, NT], F32))
        p_rstd = A(nc.sbuf_tensor("p_rstd", [128, NT], F32))
        dwg = A(nc.semaphore("dwg"))
        dws = A(nc.semaphore("dws"))

        gcnt = 0
        tcnt = 0
        with nc.sbuf_tensor("wraw", [128, NDC, D], BF16) as wraw:
            for wi, (w_dram, w_t) in enumerate(
                    ((wqd, wqT), (wkd, wkT), (wvd, wvT), (wod, woT))):
                with nc.Block() as blk:
                    @blk.gpsimd
                    def _(g, w_dram=w_dram, base=gcnt):
                        g.dma_start(
                            wraw[:],
                            w_dram.ap().rearrange("(c p) d -> p c d", p=128),
                        ).then_inc(dwg, 16)
                        g.wait_ge(dwg, base + 16)
                gcnt += 16
                with nc.Block() as blk:
                    @blk.sync
                    def _(sync, w_t=w_t, base=tcnt):
                        sync.dma_start(w_t[:], wraw[:], transpose=True).then_inc(dws, 16)
                        sync.wait_ge(dws, base + 16)
                tcnt += 16
        with nc.Block() as blk:
            @blk.gpsimd
            def _(g, base=gcnt):
                g.dma_start(bo_bf[:], bod.ap()).then_inc(dwg, 16)
                g.wait_ge(dwg, base + 16)
            @blk.vector
            def _(v):
                v.memset(ones_r[:], 1.0)
                v.memset(ones_c[:], 1.0)
        gcnt += 16

        sx = DmaSlots(nc, A, "sx", 2)
        ssq = A(nc.semaphore("ssq"))
        ssum = A(nc.semaphore("ssum"))
        with nc.Block() as blk:
            emit_ln_stats_phase(blk, tile_src, NTF, xt, sx, ssq, ssum,
                                s_sumsq, s_sum, scr_sq, scr_cp)
        with nc.Block() as blk:
            emit_stat_math_phase(blk, s_sum, s_sumsq, s_negm, s_var, s_rstd, NTF, D)
        with nc.Block() as blk:
            emit_rstd_phase(blk, s_var, s_rstd, NTF)

        xf_own = A(nc.sbuf_tensor("xf_own", [128, NT, NDC, 128], BF16))
        qf_fm = A(nc.sbuf_tensor("qf_fm", [128, NDC, S_CORE], BF16))
        kv2 = A(nc.sbuf_tensor("kv2", [128, NP, 128], BF16))
        ks_bd = A(nc.sbuf_tensor("ks_bd", [128, NH], BF16))

        with contextlib.ExitStack() as e3:
            A3 = e3.enter_context
            xln = A3(nc.sbuf_tensor("xln", [128, 2, D], BF16))
            xf = A3(nc.sbuf_tensor("xf", [128, 2, NDC, 128], BF16))
            scrA = A3(nc.sbuf_tensor("scrA", [128, 2, D], F32))
            scrB = A3(nc.sbuf_tensor("scrB", [128, 2, D], F32))
            scrC = A3(nc.sbuf_tensor("scrC", [128, 2, D], F32))
            kf_t = A3(nc.sbuf_tensor("kf_t", [128, 2, D], BF16))
            v_t = A3(nc.sbuf_tensor("v_t", [128, 2, D], BF16))
            sxt = DmaSlots(nc, A3, "sxt", 2)
            sxln = A3(nc.semaphore("sxln"))
            sxf = DmaSlots(nc, A3, "sxf", 2)
            skmm = A3(nc.semaphore("skmm"))
            svmm = A3(nc.semaphore("svmm"))
            sminev = A3(nc.semaphore("sminev"))
            skev = A3(nc.semaphore("skev"))
            sExp = A3(nc.semaphore("sExp"))
            skf = A3(nc.semaphore("skf"))
            svev = A3(nc.semaphore("svev"))
            skvmm = A3(nc.semaphore("skvmm"))
            kps = A3(nc.psum_tensor("kps", [128, D], F32))
            vps = A3(nc.psum_tensor("vps", [128, D], F32))
            kvps = A3(nc.psum_tensor("kvps", [128, NP, 128], F32))
            ksps = A3(nc.psum_tensor("ksps", [128, NDC], F32))
            bstk = contextlib.ExitStack()
            blk = bstk.enter_context(nc.Block())

            @blk.gpsimd
            def _(g):
                for t in range(NTF):
                    if t >= 2:
                        g.wait_ge(sxln, t - 1)
                    sxt.inc(g.dma_start(xt[:, t % 2, :], tile_src(t)), t)

            @blk.vector
            def _(v):
                v = DrainingEngine(v)
                for t in range(NTF):
                    sxt.wait(v, t)
                    if t >= 2:
                        sxf.wait(v, t - 2)
                    v.tensor_scalar(xln[:, t % 2, :], xt[:, t % 2, :],
                                    s_negm[:, t:t + 1], s_rstd[:, t:t + 1],
                                    ALU.add, ALU.mult).then_inc(sxln, 1)
                    v.wait_ge(skmm, t + 1)
                    if t >= 2:
                        v.wait_ge(sExp, t - 1)
                    v.tensor_scalar_min(scrA[:, t % 2, :], kps[:], 0.0).then_inc(sminev, 1)
                    v.tensor_scalar_max(scrC[:, t % 2, :], kps[:], 0.0).then_inc(skev, 1)
                    v.wait_ge(sExp, t + 1)
                    if t >= 2:
                        v.wait_ge(skvmm, t - 1)
                    v.tensor_tensor(kf_t[:, t % 2, :], scrB[:, t % 2, :],
                                    scrC[:, t % 2, :], ALU.add).then_inc(skf, 1)
                    v.wait_ge(svmm, t + 1)
                    v.tensor_copy(v_t[:, t % 2, :], vps[:]).then_inc(svev, 1)

            @blk.scalar
            def _(sc):
                sc = DrainingEngine(sc)
                for t in range(NTF):
                    sc.wait_ge(sminev, t + 1)
                    if t >= 2:
                        sc.wait_ge(skf, t - 1)
                    sc.activation(scrB[:, t % 2, :], scrA[:, t % 2, :],
                                  AF.Exp).then_inc(sExp, 1)

            @blk.sync
            def _(sy_):
                for t in range(NTF):
                    sy_.wait_ge(sxln, t + 1)
                    dst = (xf_own.ap()[:, t, :, :] if t < NT
                           else xf.ap()[:, t % 2, :, :])
                    sxf.inc(sy_.dma_start(
                        dst,
                        xln.ap()[:, t % 2, :].rearrange("p (c q) -> p c q", c=NDC),
                        transpose=True,
                    ), t)

            @blk.tensor
            def _(t_):
                for t in range(NTF):
                    sxf.wait(t_, t)
                    xfv = (xf_own.ap()[:, t, :, :] if t < NT
                           else xf.ap()[:, t % 2, :, :])
                    if t >= 1:
                        t_.wait_ge(skev, t)
                    for j in range(2):
                        for c in range(NDC):
                            t_.matmul(kps[:, j * 512:(j + 1) * 512],
                                      xfv[:, c, :],
                                      wkT.ap()[:, 4 * j:4 * j + 4, c, :],
                                      start=(c == 0), stop=(c == NDC - 1))
                    t_.drain().then_inc(skmm, 1)
                    if t >= 1:
                        t_.wait_ge(svev, t)
                    for j in range(2):
                        for c in range(NDC):
                            t_.matmul(vps[:, j * 512:(j + 1) * 512],
                                      xfv[:, c, :],
                                      wvT.ap()[:, 4 * j:4 * j + 4, c, :],
                                      start=(c == 0), stop=(c == NDC - 1))
                    t_.drain().then_inc(svmm, 1)
                    if t >= 1:
                        tp = t - 1
                        t_.wait_ge(skf, tp + 1)
                        t_.wait_ge(svev, tp + 1)
                        for pp in range(NP):
                            t_.matmul(kvps[:, pp, :],
                                      kf_t[:, tp % 2, pp * 128:(pp + 1) * 128],
                                      v_t[:, tp % 2, pp * 128:(pp + 1) * 128],
                                      start=(tp == 0), stop=(tp == NTF - 1),
                                      skip_group_check=True)
                        for c in range(NDC):
                            t_.matmul(ksps[:, c:c + 1],
                                      kf_t[:, tp % 2, c * 128:(c + 1) * 128],
                                      ones_c[:],
                                      start=(tp == 0), stop=(tp == NTF - 1),
                                      skip_group_check=True)
                        t_.drain().then_inc(skvmm, 1)
                tp = NTF - 1
                t_.wait_ge(skf, tp + 1)
                t_.wait_ge(svev, tp + 1)
                for pp in range(NP):
                    t_.matmul(kvps[:, pp, :],
                              kf_t[:, tp % 2, pp * 128:(pp + 1) * 128],
                              v_t[:, tp % 2, pp * 128:(pp + 1) * 128],
                              start=False, stop=True, skip_group_check=True)
                for c in range(NDC):
                    t_.matmul(ksps[:, c:c + 1],
                              kf_t[:, tp % 2, c * 128:(c + 1) * 128],
                              ones_c[:],
                              start=False, stop=True, skip_group_check=True)
                t_.drain().then_inc(skvmm, 1)

            bstk.close()
            blk2 = bstk.enter_context(nc.Block())

            @blk2.vector
            def _(v):
                v = DrainingEngine(v)
                v.memset(kv2[:], 0.0)
                v.memset(ks_bd[:], 0.0)
                for pp in range(NP):
                    v.tensor_copy(kv2[0:64, pp, 0:64], kvps[0:64, pp, 0:64])
                    v.tensor_copy(kv2[64:128, pp, 64:128], kvps[64:128, pp, 64:128])
                    v.tensor_copy(ks_bd[0:64, 2 * pp:2 * pp + 1], ksps[0:64, pp:pp + 1])
                    v.tensor_copy(ks_bd[64:128, 2 * pp + 1:2 * pp + 2],
                                  ksps[64:128, pp:pp + 1])
            bstk.close()

        with contextlib.ExitStack() as e4:
            A4 = e4.enter_context
            scrA = A4(nc.sbuf_tensor("q_scrA", [128, 2, 512], F32))
            scrB = A4(nc.sbuf_tensor("q_scrB", [128, 2, 512], F32))
            scrC = A4(nc.sbuf_tensor("q_scrC", [128, 2, 512], F32))
            sqmm = A4(nc.semaphore("sqmm"))
            sminev = A4(nc.semaphore("q_sminev"))
            sqev = A4(nc.semaphore("q_sqev"))
            sExp = A4(nc.semaphore("q_sExp"))
            sqf = A4(nc.semaphore("q_sqf"))
            qps = A4(nc.psum_tensor("qps", [128, 2, 512], F32))
            blk = A4(nc.Block())
            NSLC = S_CORE // 512

            @blk.tensor
            def _(t_):
                k = 0
                for co in range(NDC):
                    for st in range(NSLC):
                        if k >= 2:
                            t_.wait_ge(sqev, k - 1)
                        for ci in range(NDC):
                            t_.matmul(qps[:, k % 2, :],
                                      wqT.ap()[:, co, ci, :],
                                      xf_own.ap()[:, 4 * st:4 * st + 4, ci, :],
                                      start=(ci == 0), stop=(ci == NDC - 1))
                        t_.drain().then_inc(sqmm, 1)
                        k += 1

            @blk.vector
            def _(v):
                v = DrainingEngine(v)
                k = 0
                for co in range(NDC):
                    for st in range(NSLC):
                        v.wait_ge(sqmm, k + 1)
                        if k >= 2:
                            v.wait_ge(sExp, k - 1)
                        v.tensor_scalar_min(scrA[:, k % 2, :], qps[:, k % 2, :],
                                            0.0).then_inc(sminev, 1)
                        v.tensor_scalar_max(scrC[:, k % 2, :], qps[:, k % 2, :],
                                            0.0).then_inc(sqev, 1)
                        v.wait_ge(sExp, k + 1)
                        v.tensor_tensor(qf_fm[:, co, st * 512:(st + 1) * 512],
                                        scrB[:, k % 2, :], scrC[:, k % 2, :],
                                        ALU.add).then_inc(sqf, 1)
                        k += 1

            @blk.scalar
            def _(sc):
                sc = DrainingEngine(sc)
                for k in range(NDC * NSLC):
                    sc.wait_ge(sminev, k + 1)
                    if k >= 2:
                        sc.wait_ge(sqf, k - 1)
                    sc.activation(scrB[:, k % 2, :], scrA[:, k % 2, :],
                                  AF.Exp).then_inc(sExp, 1)

        with contextlib.ExitStack() as e5:
            A5 = e5.enter_context
            att = A5(nc.sbuf_tensor("att", [128, 2, D], BF16))
            af = A5(nc.sbuf_tensor("af", [128, 2, NDC, 128], BF16))
            dnv = A5(nc.sbuf_tensor("dnv", [128, 2, NH], F32))
            y_sb = A5(nc.sbuf_tensor("y_sb", [128, 2, D], BF16))
            sden = A5(nc.semaphore("sden"))
            sdev = A5(nc.semaphore("sdev"))
            satt = A5(nc.semaphore("satt"))
            saf = DmaSlots(nc, A5, "saf", 2)
            sop = A5(nc.semaphore("sop"))
            sy2 = A5(nc.semaphore("sy2"))
            spsq = A5(nc.semaphore("spsq"))
            syst = DmaSlots(nc, A5, "syst", 2)
            dps_d = A5(nc.psum_tensor("dps_d", [128, 2, NH], F32))
            qk_ps = A5(nc.psum_tensor("qk_ps", [128, 2, NP, 128], F32))
            ops = A5(nc.psum_tensor("ops", [128, D], F32))
            blk = A5(nc.Block())

            @blk.tensor
            def _(t_):
                for i in range(NT):
                    sl = slice(i * 128, (i + 1) * 128)
                    if i >= 2:
                        t_.wait_ge(sdev, i - 1)
                        t_.wait_ge(satt, i - 1)
                    for c in range(NDC):
                        t_.matmul(dps_d[:, i % 2, 2 * c:2 * c + 2],
                                  qf_fm[:, c, sl],
                                  ks_bd[:, 2 * c:2 * c + 2],
                                  start=True, stop=True)
                        t_.matmul(qk_ps[:, i % 2, c, :],
                                  qf_fm[:, c, sl],
                                  kv2[:, c, :],
                                  start=True, stop=True)
                    t_.drain().then_inc(sden, 1)
                    if i >= 1:
                        ip = i - 1
                        saf.wait(t_, ip)
                        if ip >= 1:
                            t_.wait_ge(sy2, ip)
                        for j in range(2):
                            t_.matmul(ops[:, j * 512:(j + 1) * 512],
                                      ones_r[:], bo_bf.ap()[0:1, j * 512:(j + 1) * 512],
                                      start=True, stop=False)
                            for c in range(NDC):
                                t_.matmul(ops[:, j * 512:(j + 1) * 512],
                                          af[:, ip % 2, c, :],
                                          woT.ap()[:, 4 * j:4 * j + 4, c, :],
                                          start=False, stop=(c == NDC - 1))
                        t_.drain().then_inc(sop, 1)
                ip = NT - 1
                saf.wait(t_, ip)
                t_.wait_ge(sy2, ip)
                for j in range(2):
                    t_.matmul(ops[:, j * 512:(j + 1) * 512],
                              ones_r[:], bo_bf.ap()[0:1, j * 512:(j + 1) * 512],
                              start=True, stop=False)
                    for c in range(NDC):
                        t_.matmul(ops[:, j * 512:(j + 1) * 512],
                                  af[:, ip % 2, c, :],
                                  woT.ap()[:, 4 * j:4 * j + 4, c, :],
                                  start=False, stop=(c == NDC - 1))
                t_.drain().then_inc(sop, 1)

            @blk.vector
            def _(v):
                v = DrainingEngine(v)
                for i in range(NT):
                    v.wait_ge(sden, i + 1)
                    v.tensor_scalar_add(dnv[:, i % 2, :], dps_d[:, i % 2, :], EPS)
                    v.reciprocal(dnv[:, i % 2, :], dnv[:, i % 2, :]).then_inc(sdev, 1)
                    if i >= 2:
                        saf.wait(v, i - 2)
                    v.tensor_tensor(
                        att.ap()[:, i % 2, :].rearrange("p (c t q) -> p c t q", c=NP, t=2),
                        qk_ps.ap()[:, i % 2, :, :].rearrange("p c (t q) -> p c t q", t=2),
                        dnv.ap()[:, i % 2, :].rearrange("p (c t) -> p c t", c=NP)
                        .unsqueeze(3).broadcast_to([128, NP, 2, 64]),
                        ALU.mult).then_inc(satt, 1)
                    v.wait_ge(sop, i + 1)
                    if i >= 2:
                        syst.wait(v, i - 2)
                        v.wait_ge(spsq, i - 1)
                    v.tensor_scalar(y_sb[:, i % 2, :], ops[:], 1.0, None,
                                    ALU.mult, ALU.add,
                                    accum_out=p_sum[:, i:i + 1]).then_inc(sy2, 1)

            @blk.scalar
            def _(sc):
                sc = DrainingEngine(sc)
                for i in range(NT):
                    sc.wait_ge(sy2, i + 1)
                    sc.activation(scr_sq[:, i % 2, :], y_sb[:, i % 2, :], AF.Square,
                                  accum_out=p_sumsq[:, i:i + 1]).then_inc(spsq, 1)

            @blk.sync
            def _(sy_):
                for i in range(NT):
                    sy_.wait_ge(satt, i + 1)
                    saf.inc(sy_.dma_start(
                        af.ap()[:, i % 2, :, :],
                        att.ap()[:, i % 2, :].rearrange("p (c q) -> p c q", c=NDC),
                        transpose=True,
                    ), i)
                    sy_.wait_ge(sy2, i + 1)
                    syst.inc(sy_.dma_start(y_scr.ap()[i * 128:(i + 1) * 128, :],
                                           y_sb[:, i % 2, :]), i)
                for k in range(2):
                    sy_.wait_ge(syst.sems[k], 16 * ((NT + 1 - k) // 2))

        with nc.Block() as blk:
            emit_stat_math_phase(blk, p_sum, p_sumsq, p_negm, p_var, p_rstd, NT, D)
        with nc.Block() as blk:
            emit_rstd_phase(blk, p_var, p_rstd, NT)
        with contextlib.ExitStack() as es6:
            emit_residual_phase(nc, es6.enter_context,
                                lambda i: x_in.ap()[i * 128:(i + 1) * 128, :],
                                lambda i: y_scr.ap()[i * 128:(i + 1) * 128, :],
                                lambda i: x_out.ap()[i * 128:(i + 1) * 128, :],
                                NT, xt, p_negm, p_rstd)

    return nc




def build_l1_attn():
    NTF = 2 * NT
    SM = 1.0 / 32.0

    nc = bass.Bass("TRN2", target_bir_lowering=False)
    x_in = nc.dram_tensor("x_in", [S_CORE, D], F32, kind="ExternalInput")
    x_oth = nc.dram_tensor("x_oth", [S_CORE, D], F32, kind="ExternalInput")
    w1d = nc.dram_tensor("w1", [D, D], F32, kind="ExternalInput")
    w2d = nc.dram_tensor("w2", [D, D], F32, kind="ExternalInput")
    w3d = nc.dram_tensor("w3", [D, D], F32, kind="ExternalInput")
    w4d = nc.dram_tensor("w4", [D, D], F32, kind="ExternalInput")
    b1d = nc.dram_tensor("b1", [1, D], F32, kind="ExternalInput")
    b2d = nc.dram_tensor("b2", [1, D], F32, kind="ExternalInput")
    b3d = nc.dram_tensor("b3", [1, D], F32, kind="ExternalInput")
    b4d = nc.dram_tensor("b4", [1, D], F32, kind="ExternalInput")
    x_out = nc.dram_tensor("x_out", [S_CORE, D], F32, kind="ExternalOutput")
    y_scr = nc.dram_tensor("y_scr", [S_CORE, D], BF16)
    q_scr = nc.dram_tensor("q_scr", [NT, 128, NDC * 128], BF16)

    def tile_src(t):
        if t < NT:
            return x_in.ap()[t * 128:(t + 1) * 128, :]
        return x_oth.ap()[(t - NT) * 128:(t - NT + 1) * 128, :]

    with contextlib.ExitStack() as es:
        A = es.enter_context
        K_fm = A(nc.sbuf_tensor("K_fm", [128, NTF, NDC, 128], BF16))
        V_tm = A(nc.sbuf_tensor("V_tm", [128, NTF, D], BF16))
        b_bf = A(nc.sbuf_tensor("b_bf", [1, 4, D], BF16))
        ones_r = A(nc.sbuf_tensor("ones_r", [1, 128], BF16))
        xt = A(nc.sbuf_tensor("xt", [128, 2, D], F32))
        scr_sq = A(nc.sbuf_tensor("scr_sq", [128, 1, D], F32))
        scr_cp = A(nc.sbuf_tensor("scr_cp", [128, 1, D], BF16))
        s_sum = A(nc.sbuf_tensor("s_sum", [128, NTF], F32))
        s_sumsq = A(nc.sbuf_tensor("s_sumsq", [128, NTF], F32))
        s_negm = A(nc.sbuf_tensor("s_negm", [128, NTF], F32))
        s_var = A(nc.sbuf_tensor("s_var", [128, NTF], F32))
        s_rstd = A(nc.sbuf_tensor("s_rstd", [128, NTF], F32))
        p_sum = A(nc.sbuf_tensor("p_sum", [128, NT], F32))
        p_sumsq = A(nc.sbuf_tensor("p_sumsq", [128, NT], F32))
        p_negm = A(nc.sbuf_tensor("p_negm", [128, NT], F32))
        p_var = A(nc.sbuf_tensor("p_var", [128, NT], F32))
        p_rstd = A(nc.sbuf_tensor("p_rstd", [128, NT], F32))
        dwg = A(nc.semaphore("dwg"))

        with nc.Block() as blk:
            @blk.gpsimd
            def _(g):
                for bi, bd in enumerate((b1d, b2d, b3d, b4d)):
                    g.dma_start(b_bf[0:1, bi, :], bd.ap()).then_inc(dwg, 16)
                g.wait_ge(dwg, 64)
            @blk.vector
            def _(v):
                v.memset(ones_r[:], 1.0)

        sx = DmaSlots(nc, A, "sx", 2)
        ssq = A(nc.semaphore("ssq"))
        ssum = A(nc.semaphore("ssum"))
        with nc.Block() as blk:
            emit_ln_stats_phase(blk, tile_src, NTF, xt, sx, ssq, ssum,
                                s_sumsq, s_sum, scr_sq, scr_cp)
        with nc.Block() as blk:
            emit_stat_math_phase(blk, s_sum, s_sumsq, s_negm, s_var, s_rstd, NTF, D)
        with nc.Block() as blk:
            emit_rstd_phase(blk, s_var, s_rstd, NTF)

        def emit_proj_phase(A3, tiles, q_mode):
            if q_mode:
                w1 = A3(nc.sbuf_tensor("w1b", [128, NDC, D], BF16))
                wlist = [(w1d, w1)]
            else:
                w2 = A3(nc.sbuf_tensor("w2b", [128, NDC, D], BF16))
                w3 = A3(nc.sbuf_tensor("w3b", [128, NDC, D], BF16))
                wlist = [(w2d, w2), (w3d, w3)]
            pref = "q" if q_mode else "kv"
            dwl = A3(nc.semaphore(pref + "_dwl"))
            with nc.Block() as blk:
                @blk.gpsimd
                def _(g):
                    for wi, (wd_, wt) in enumerate(wlist):
                        g.dma_start(wt[:],
                                    wd_.ap().rearrange("(c p) o -> p c o", p=128)
                                    ).then_inc(dwl, 16)
                    g.wait_ge(dwl, 16 * len(wlist))

            xln = A3(nc.sbuf_tensor(pref + "_xln", [128, 2, D], BF16))
            xf = A3(nc.sbuf_tensor(pref + "_xf", [128, 2, NDC, 128], BF16))
            sxt = DmaSlots(nc, A3, pref + "_sxt", 2)
            sxln = A3(nc.semaphore(pref + "_sxln"))
            sxf = DmaSlots(nc, A3, pref + "_sxf", 2)
            if q_mode:
                qtile = A3(nc.sbuf_tensor("qtile", [128, 2, NDC * 128], BF16))
                sqmm = A3(nc.semaphore("sqmm"))
                sqev = A3(nc.semaphore("sqev"))
                sqst = DmaSlots(nc, A3, "sqst", 2)
                qps = A3(nc.psum_tensor("qps", [128, 2, NDC, 128], F32))
            else:
                skmm = A3(nc.semaphore("skmm"))
                skev = A3(nc.semaphore("skev"))
                svmm = A3(nc.semaphore("svmm"))
                svev = A3(nc.semaphore("svev"))
                kps = A3(nc.psum_tensor("kps", [128, 2, NDC, 128], F32))
                vps = A3(nc.psum_tensor("vps", [128, D], F32))
            blk = A3(nc.Block())

            @blk.gpsimd
            def _(g):
                for t in tiles:
                    ti = tiles.index(t)
                    if ti >= 2:
                        g.wait_ge(sxln, ti - 1)
                    sxt.inc(g.dma_start(xt[:, ti % 2, :], tile_src(t)), ti)

            @blk.vector
            def _(v):
                v = DrainingEngine(v)
                for ti, t in enumerate(tiles):
                    sxt.wait(v, ti)
                    if ti >= 2:
                        sxf.wait(v, ti - 2)
                    v.tensor_scalar(xln[:, ti % 2, :], xt[:, ti % 2, :],
                                    s_negm[:, t:t + 1], s_rstd[:, t:t + 1],
                                    ALU.add, ALU.mult).then_inc(sxln, 1)
                    if q_mode:
                        v.wait_ge(sqmm, ti + 1)
                        if ti >= 2:
                            sqst.wait(v, ti - 2)
                        v.tensor_copy(qtile[:, ti % 2, :],
                                      qps.ap()[:, ti % 2, :, :]
                                      .rearrange("p a b -> p (a b)")).then_inc(sqev, 1)
                    else:
                        v.wait_ge(skmm, ti + 1)
                        v.tensor_copy(K_fm.ap()[:, t, :, :]
                                      .rearrange("p a b -> p (a b)"),
                                      kps.ap()[:, ti % 2, :, :]
                                      .rearrange("p a b -> p (a b)")).then_inc(skev, 1)
                        v.wait_ge(svmm, ti + 1)
                        v.tensor_copy(V_tm[:, t, :], vps[:]).then_inc(svev, 1)

            @blk.sync
            def _(sy_):
                for ti, t in enumerate(tiles):
                    sy_.wait_ge(sxln, ti + 1)
                    sxf.inc(sy_.dma_start(
                        xf.ap()[:, ti % 2, :, :],
                        xln.ap()[:, ti % 2, :].rearrange("p (c q) -> p c q", c=NDC),
                        transpose=True,
                    ), ti)
                    if q_mode:
                        sy_.wait_ge(sqev, ti + 1)
                        sqst.inc(sy_.dma_start(q_scr.ap()[t, :, :],
                                               qtile[:, ti % 2, :]), ti)
                if q_mode:
                    for k in range(2):
                        sy_.wait_ge(sqst.sems[k], 16 * ((len(tiles) + 1 - k) // 2))

            @blk.tensor
            def _(t_):
                for ti, t in enumerate(tiles):
                    sxf.wait(t_, ti)
                    if q_mode:
                        if ti >= 2:
                            t_.wait_ge(sqev, ti - 1)
                        for ce in range(NDC):
                            t_.matmul(qps[:, ti % 2, ce, :], ones_r[:],
                                      b_bf.ap()[0:1, 0, ce * 128:(ce + 1) * 128],
                                      start=True, stop=False)
                            for ci in range(NDC):
                                t_.matmul(qps[:, ti % 2, ce, :],
                                          w1[:, ci, ce * 128:(ce + 1) * 128],
                                          xf[:, ti % 2, ci, :],
                                          start=False, stop=(ci == NDC - 1))
                        t_.drain().then_inc(sqmm, 1)
                    else:
                        if ti >= 2:
                            t_.wait_ge(skev, ti - 1)
                        for ce in range(NDC):
                            t_.matmul(kps[:, ti % 2, ce, :], ones_r[:],
                                      b_bf.ap()[0:1, 1, ce * 128:(ce + 1) * 128],
                                      start=True, stop=False)
                            for ci in range(NDC):
                                t_.matmul(kps[:, ti % 2, ce, :],
                                          w2[:, ci, ce * 128:(ce + 1) * 128],
                                          xf[:, ti % 2, ci, :],
                                          start=False, stop=(ci == NDC - 1))
                        t_.drain().then_inc(skmm, 1)
                        if ti >= 1:
                            t_.wait_ge(svev, ti)
                        for j in range(2):
                            t_.matmul(vps[:, j * 512:(j + 1) * 512], ones_r[:],
                                      b_bf.ap()[0:1, 2, j * 512:(j + 1) * 512],
                                      start=True, stop=False)
                            for ci in range(NDC):
                                t_.matmul(vps[:, j * 512:(j + 1) * 512],
                                          xf[:, ti % 2, ci, :],
                                          w3[:, ci, j * 512:(j + 1) * 512],
                                          start=False, stop=(ci == NDC - 1))
                        t_.drain().then_inc(svmm, 1)

        with contextlib.ExitStack() as e1:
            emit_proj_phase(e1.enter_context, list(range(NTF)), False)
        with contextlib.ExitStack() as e1:
            emit_proj_phase(e1.enter_context, list(range(NT)), True)

        with contextlib.ExitStack() as e2:
            A2 = e2.enter_context
            w4 = A2(nc.sbuf_tensor("w4b", [128, NDC, D], BF16))
            dwl = A2(nc.semaphore("c2_dwl"))
            with nc.Block() as blk:
                @blk.gpsimd
                def _(g):
                    g.dma_start(w4[:], w4d.ap().rearrange("(c p) o -> p c o", p=128)
                                ).then_inc(dwl, 16)
                    g.wait_ge(dwl, 16)

            qt = A2(nc.sbuf_tensor("qt", [128, 2, NDC, 128], BF16))
            sc_sb = A2(nc.sbuf_tensor("sc_sb", [128, S], F32))
            mx8 = A2(nc.sbuf_tensor("mx8", [128, 8], F32))
            bias_c = A2(nc.sbuf_tensor("bias_c", [128, 1], F32))
            l4 = A2(nc.sbuf_tensor("l4", [128, 4], F32))
            rl = A2(nc.sbuf_tensor("rl", [128, 1], F32))
            P = A2(nc.sbuf_tensor("P", [128, 2, 1024], BF16))
            PT = A2(nc.sbuf_tensor("PT", [128, 2, 8, 128], BF16))
            att = A2(nc.sbuf_tensor("att", [128, 2, D], BF16))
            af = A2(nc.sbuf_tensor("af", [128, 2, NDC, 128], BF16))
            y_sb = A2(nc.sbuf_tensor("y_sb", [128, 2, D], BF16))
            sq = DmaSlots(nc, A2, "c2_sq", 2)
            ssc = A2(nc.semaphore("ssc"))
            ssce = A2(nc.semaphore("ssce"))
            sbias = A2(nc.semaphore("sbias"))
            sexp = A2(nc.semaphore("sexp"))
            spt = DmaSlots(nc, A2, "c2_spt", 2)
            samm = A2(nc.semaphore("samm"))
            sptd = A2(nc.semaphore("sptd"))
            srl = A2(nc.semaphore("srl"))
            satt = A2(nc.semaphore("satt"))
            saf = DmaSlots(nc, A2, "c2_saf", 2)
            sop = A2(nc.semaphore("sop"))
            sy2 = A2(nc.semaphore("sy2"))
            spsq = A2(nc.semaphore("spsq"))
            syst = DmaSlots(nc, A2, "c2_syst", 2)
            sc_ps = A2(nc.psum_tensor("sc_ps", [128, 2, 512], F32))
            aps = A2(nc.psum_tensor("aps", [128, D], F32))
            ops = A2(nc.psum_tensor("ops", [128, D], F32))
            blk = A2(nc.Block())

            @blk.gpsimd
            def _(g):
                for i in range(NT):
                    if i >= 2:
                        g.wait_ge(samm, i - 1)
                    sq.inc(g.dma_start(
                        qt.ap()[:, i % 2, :, :].rearrange("p a b -> p (a b)"),
                        q_scr.ap()[i, :, :]), i)

            @blk.tensor
            def _(t_):
                for i in range(NT):
                    sq.wait(t_, i)
                    for j in range(8):
                        gidx = i * 8 + j
                        if gidx >= 2:
                            t_.wait_ge(ssce, gidx - 1)
                        for ce in range(NDC):
                            t_.matmul(sc_ps[:, j % 2, :],
                                      qt[:, i % 2, ce, :],
                                      K_fm.ap()[:, 4 * j:4 * j + 4, ce, :],
                                      start=(ce == 0), stop=(ce == NDC - 1))
                        t_.drain().then_inc(ssc, 1)
                    if i >= 1:
                        t_.wait_ge(satt, i)
                    for c in range(4):
                        spt.wait(t_, i * 4 + c)
                        for tt in range(8):
                            for j2 in range(2):
                                t_.matmul(aps[:, j2 * 512:(j2 + 1) * 512],
                                          PT[:, c % 2, tt, :],
                                          V_tm[:, 8 * c + tt,
                                               j2 * 512:(j2 + 1) * 512],
                                          start=(c == 0 and tt == 0),
                                          stop=(c == 3 and tt == 7),
                                          skip_group_check=True)
                        t_.drain().then_inc(sptd, 1)
                    t_.nop().then_inc(samm, 1)
                    if i >= 1:
                        ip = i - 1
                        saf.wait(t_, ip)
                        if ip >= 1:
                            t_.wait_ge(sy2, ip)
                        for j in range(2):
                            t_.matmul(ops[:, j * 512:(j + 1) * 512], ones_r[:],
                                      b_bf.ap()[0:1, 3, j * 512:(j + 1) * 512],
                                      start=True, stop=False)
                            for ce in range(NDC):
                                t_.matmul(ops[:, j * 512:(j + 1) * 512],
                                          af[:, ip % 2, ce, :],
                                          w4[:, ce, j * 512:(j + 1) * 512],
                                          start=False, stop=(ce == NDC - 1))
                        t_.drain().then_inc(sop, 1)
                ip = NT - 1
                saf.wait(t_, ip)
                t_.wait_ge(sy2, ip)
                for j in range(2):
                    t_.matmul(ops[:, j * 512:(j + 1) * 512], ones_r[:],
                              b_bf.ap()[0:1, 3, j * 512:(j + 1) * 512],
                              start=True, stop=False)
                    for ce in range(NDC):
                        t_.matmul(ops[:, j * 512:(j + 1) * 512],
                                  af[:, ip % 2, ce, :],
                                  w4[:, ce, j * 512:(j + 1) * 512],
                                  start=False, stop=(ce == NDC - 1))
                t_.drain().then_inc(sop, 1)

            @blk.vector
            def _(v):
                v = DrainingEngine(v)

                def emit_yev(v, i):
                    v.wait_ge(sop, i + 1)
                    if i >= 2:
                        syst.wait(v, i - 2)
                        v.wait_ge(spsq, i - 1)
                    v.tensor_scalar(y_sb[:, i % 2, :], ops[:], 1.0, None,
                                    ALU.mult, ALU.add,
                                    accum_out=p_sum[:, i:i + 1]).then_inc(sy2, 1)
                for i in range(NT):
                    for j in range(8):
                        gidx = i * 8 + j
                        v.wait_ge(ssc, gidx + 1)
                        v.tensor_copy(sc_sb[:, j * 512:(j + 1) * 512],
                                      sc_ps[:, j % 2, :])
                        v.reduce_max(mx8[:, j:j + 1], sc_sb[:, j * 512:(j + 1) * 512],
                                     axis=AX).then_inc(ssce, 1)
                    v.reduce_max(bias_c[:], mx8[:], axis=AX)
                    v.tensor_scalar_mul(bias_c[:], bias_c[:], -SM)
                    v.raw.drain().then_inc(sbias, 1)
                    v.wait_ge(sexp, 4 * (i + 1))
                    v.reduce_sum(rl[:], l4[:], axis=AX)
                    v.reciprocal(rl[:], rl[:]).then_inc(srl, 1)
                    v.wait_ge(samm, i + 1)
                    if i >= 2:
                        saf.wait(v, i - 2)
                    v.tensor_scalar_mul(att[:, i % 2, :], aps[:], rl[:, 0:1]
                                        ).then_inc(satt, 1)
                    if i >= 1:
                        emit_yev(v, i - 1)
                if NT >= 1:
                    emit_yev(v, NT - 1)

            @blk.scalar
            def _(sc):
                sc = DrainingEngine(sc)
                def emit_sq(i):
                    sc.wait_ge(sy2, i + 1)
                    sc.activation(scr_sq[:, 0, :], y_sb[:, i % 2, :], AF.Square,
                                  accum_out=p_sumsq[:, i:i + 1]).then_inc(spsq, 1)
                for i in range(NT):
                    sc.wait_ge(sbias, i + 1)
                    for c in range(4):
                        if i * 4 + c >= 2:
                            spt.wait(sc, i * 4 + c - 2)
                        sc.activation(P[:, c % 2, :], sc_sb[:, c * 1024:(c + 1) * 1024],
                                      AF.Exp, bias=bias_c[:, 0:1], scale=SM,
                                      accum_out=l4[:, c:c + 1]).then_inc(sexp, 1)
                    if i >= 1:
                        emit_sq(i - 1)
                emit_sq(NT - 1)

            @blk.sync
            def _(sy_):
                def emit_af(i):
                    sy_.wait_ge(satt, i + 1)
                    saf.inc(sy_.dma_start(
                        af.ap()[:, i % 2, :, :],
                        att.ap()[:, i % 2, :].rearrange("p (c q) -> p c q", c=NDC),
                        transpose=True,
                    ), i)

                def emit_yst(i):
                    sy_.wait_ge(sy2, i + 1)
                    syst.inc(sy_.dma_start(y_scr.ap()[i * 128:(i + 1) * 128, :],
                                           y_sb[:, i % 2, :]), i)
                for i in range(NT):
                    for c in range(4):
                        sy_.wait_ge(sexp, i * 4 + c + 1)
                        if i * 4 + c >= 2:
                            sy_.wait_ge(sptd, i * 4 + c - 1)
                        spt.inc(sy_.dma_start(
                            PT.ap()[:, c % 2, :, :],
                            P.ap()[:, c % 2, :].rearrange("p (a b) -> p a b", a=NDC),
                            transpose=True,
                        ), i * 4 + c)
                    emit_af(i)
                    if i >= 1:
                        emit_yst(i - 1)
                emit_yst(NT - 1)
                for k in range(2):
                    sy_.wait_ge(syst.sems[k], 16 * ((NT + 1 - k) // 2))

        with nc.Block() as blk:
            emit_stat_math_phase(blk, p_sum, p_sumsq, p_negm, p_var, p_rstd, NT, D)
        with nc.Block() as blk:
            emit_rstd_phase(blk, p_var, p_rstd, NT)
        with contextlib.ExitStack() as es6:
            emit_residual_phase(nc, es6.enter_context,
                                lambda i: x_in.ap()[i * 128:(i + 1) * 128, :],
                                lambda i: y_scr.ap()[i * 128:(i + 1) * 128, :],
                                lambda i: x_out.ap()[i * 128:(i + 1) * 128, :],
                                NT, xt, p_negm, p_rstd)

    return nc



_NC_CACHE = {}


def _get_nc(name, builder):
    if name not in _NC_CACHE:
        _NC_CACHE[name] = builder()
    return _NC_CACHE[name]


def _scatter_x(x):
    out = []
    for core in range(8):
        b, half = core // 2, core % 2
        out.append(np.ascontiguousarray(
            x[b, half * S_CORE:(half + 1) * S_CORE, :], dtype=np.float32))
    return out


def _gather_x(res, key="x_out"):
    out = np.empty((B, S, D), dtype=np.float32)
    for core in range(8):
        b, half = core // 2, core % 2
        out[b, half * S_CORE:(half + 1) * S_CORE, :] = res.results[core][key]
    return out


def _run_l1_ffn(x, wg, wu, wd):
    nc = _get_nc("l1_ffn", build_l1_ffn)
    wg = np.ascontiguousarray(wg, dtype=np.int32)
    wu = np.ascontiguousarray(wu, dtype=np.int32)
    wd = np.ascontiguousarray(wd, dtype=np.int32)
    xs = _scatter_x(x)
    in_maps = [{"x_in": xs[c], "wg": wg, "wu": wu, "wd": wd} for c in range(8)]
    res = run_bass_kernel_spmd(nc, in_maps, core_ids=list(range(8)))
    return _gather_x(res)


def _run_l0_ffn(x, wg, wu, wd):
    nc = _get_nc("l0_ffn", build_l0_ffn)
    wg = np.ascontiguousarray(wg, dtype=np.float32)
    wu = np.ascontiguousarray(wu, dtype=np.float32)
    wd = np.ascontiguousarray(wd, dtype=np.float32)
    xs = _scatter_x(x)
    in_maps = [{"x_in": xs[c], "wg": wg, "wu": wu, "wd": wd} for c in range(8)]
    res = run_bass_kernel_spmd(nc, in_maps, core_ids=list(range(8)))
    return _gather_x(res)


def _run_l0_attn(x, wq, wk, wv, wo, bo):
    nc = _get_nc("l0_attn", build_l0_attn)
    wq = np.ascontiguousarray(wq, dtype=np.float32)
    wk = np.ascontiguousarray(wk, dtype=np.float32)
    wv = np.ascontiguousarray(wv, dtype=np.float32)
    wo = np.ascontiguousarray(wo, dtype=np.float32)
    bo = np.ascontiguousarray(bo, dtype=np.float32).reshape(1, D)
    xs = _scatter_x(x)
    in_maps = []
    for c in range(8):
        oth = c ^ 1
        in_maps.append({"x_in": xs[c], "x_oth": xs[oth], "wq": wq, "wk": wk,
                        "wv": wv, "wo": wo, "bo": bo})
    res = run_bass_kernel_spmd(nc, in_maps, core_ids=list(range(8)))
    return _gather_x(res)


def _run_l1_attn(x, w1, b1, w2, b2, w3, b3, w4, b4):
    nc = _get_nc("l1_attn", build_l1_attn)
    ws = [np.ascontiguousarray(w, dtype=np.float32) for w in (w1, w2, w3, w4)]
    bs = [np.ascontiguousarray(b, dtype=np.float32).reshape(1, D) for b in (b1, b2, b3, b4)]
    xs = _scatter_x(x)
    in_maps = []
    for c in range(8):
        m = {"x_in": xs[c], "x_oth": xs[c ^ 1]}
        for k in range(4):
            m[f"w{k+1}"] = ws[k]
            m[f"b{k+1}"] = bs[k]
        in_maps.append(m)
    res = run_bass_kernel_spmd(nc, in_maps, core_ids=list(range(8)))
    return _gather_x(res)


def kernel(**inputs):
    x = np.asarray(inputs["L0_attn_x"], dtype=np.float32)
    x = _run_l0_attn(x, inputs["L0_attn_wq"], inputs["L0_attn_wk"],
                     inputs["L0_attn_wv"], inputs["L0_attn_wo"], inputs["L0_attn_bo"])
    x = _run_l0_ffn(x, inputs["L0_ffn_w_gate"], inputs["L0_ffn_w_up"],
                    inputs["L0_ffn_w_down"])
    x = _run_l1_attn(x, inputs["L1_attn_W1"], inputs["L1_attn_b1"],
                     inputs["L1_attn_W2"], inputs["L1_attn_b2"],
                     inputs["L1_attn_W3"], inputs["L1_attn_b3"],
                     inputs["L1_attn_W4"], inputs["L1_attn_b4"])
    x = _run_l1_ffn(x, inputs["L1_ffn_w_gate"], inputs["L1_ffn_w_up"],
                    inputs["L1_ffn_w_down"])
    return x
